# version 17
# speedup vs baseline: 1.2618x; 1.0190x over previous
import numpy as np
import ml_dtypes
from contextlib import ExitStack

import concourse.bass as bass
import concourse.mybir as mybir
from concourse.bass_utils import run_bass_kernel_spmd
from concourse.bass_core_rust import IndirectOffsetOnAxis

F32 = mybir.dt.float32
BF16 = mybir.dt.bfloat16
I32 = mybir.dt.int32
AF = mybir.ActivationFunctionType
ALU = mybir.AluOpType
AX = mybir.AxisListType

D, H, DH, DS, DI, E, KC = 1024, 16, 64, 48, 2048, 8, 4
NG = D // 128
ENGINES = ("sync", "gpsimd", "vector", "scalar", "tensor")


class Sem:
    def __init__(self, ctx, nc, name):
        self.h = ctx.enter_context(nc.semaphore(name))
        self.n = 0

    def w(self):
        return (self, self.n)


class P:

    SELF = {"vector": "sDVE", "scalar": "sACT", "gpsimd": "sPOOL"}

    def __init__(self, nc, ctx):
        self.nc = nc
        self.ctx = ctx
        self.ev = {e: [] for e in ENGINES}
        self.selfsem = {}

    def sem(self, name):
        s = Sem(self.ctx, self.nc, name)
        for eng, nm in self.SELF.items():
            if nm == name:
                self.selfsem[eng] = s
        return s

    def emit(self, eng_name, fn, waits=(), inc=None, inc16=None, selfsync=True):
        waits = list(waits)
        auto = None
        if eng_name in self.selfsem and inc16 is None and selfsync:
            ss = self.selfsem[eng_name]
            waits.append((ss, ss.n))
            if inc is None or inc is not ss:
                ss.n += 1
                auto = (ss.h, 1)
        waits = tuple((s, v) for (s, v) in waits if v > 0)
        if inc is not None:
            inc.n += 1
            tgt = (inc.h, 1)
        elif inc16 is not None:
            inc16.n += 16
            tgt = (inc16.h, 16)
        else:
            tgt = None

        def run(eng):
            seen = set()
            for (s, v) in waits:
                key = (id(s.h), v)
                if key in seen:
                    continue
                seen.add(key)
                eng.wait_ge(s.h, v)
            if fn is None:
                return None
            inst = fn(eng)
            if tgt is not None:
                inst.then_inc(tgt[0], tgt[1])
            if auto is not None:
                inst.then_inc(auto[0], auto[1])
            return inst
        self.ev[eng_name].append(run)

    def dma(self, q, out, in_, sem, waits=(), **kw):
        self.emit(q, lambda e: e.dma_start(out=out, in_=in_, **kw), waits=waits, inc16=sem)

    def run_block(self):
        with self.nc.Block() as block:
            for name in ENGINES:
                evs = self.ev[name]
                if not evs:
                    continue

                def make(evlist):
                    def body(eng):
                        for f in evlist:
                            f(eng)
                    return body
                getattr(block, name)(make(evs))


def build_program(L=8192, taps=(), nphase=99):
    CAP = L // 4
    NT128 = L // 128
    NT512 = L // 512
    NFT = DI // 128

    nc = bass.Bass("TRN2", target_bir_lowering=False)
    ctx = ExitStack()

    def din(name, shape, dt=F32):
        return nc.dram_tensor(name, shape, dt, kind="ExternalInput").ap()

    x_in = din("x_in", [L, D])
    pw_wT = din("pw_wT", [D, D])
    out_projT = din("out_projT", [D, D])
    gate_wT = din("gate_wT", [D, DI])
    up_wT = din("up_wT", [D, DI])
    down_wT = din("down_wT", [DI, D])
    choice_wT = din("choice_wT", [D, E])
    xpwT = din("xpwT", [DH, 160])
    xpw_dt = din("xpw_dt", [DH, DH])
    dtwT = din("dtwT", [DH, DH])
    dt_b = din("dt_b", [DH, 1])
    A_log_t = din("A_log_t", [H, DS])
    dwk = din("dwk", [D, KC])
    dw_bias = din("dw_bias", [D, 1])
    pw_b = din("pw_b", [D, 1])
    opb = din("opb", [1, D])
    n1w = din("n1w", [D, 1])
    n2w = din("n2w", [1, D])
    w1 = din("w1", [E, D, DI])
    w2 = din("w2", [E, DI, D])
    c_ident = din("c_ident", [128, 128])
    c_onehot = din("c_onehot", [128, DS * 128])
    c_LT = din("c_LT", [128, 128])
    c_ones = din("c_ones", [128, 1])
    c_ecap = din("c_ecap", [1, E])

    out_hbm = nc.dram_tensor("out", [L, D], F32, kind="ExternalOutput").ap()

    def scr(name, shape, dt):
        kind = "ExternalOutput" if name in taps else "Internal"
        return nc.dram_tensor(name, shape, dt, kind=kind).ap()

    xn_f32 = scr("xn_f32", [L, D], F32)
    xn_b16 = scr("xn_b16", [L, D], BF16)
    probs_h = scr("probs_h", [L, E], F32)
    uc_h = scr("uc_h", [D, L], BF16)
    xh_h = scr("xh_h", [D, L], BF16)
    dt_hD = scr("dt_h", [D, L], F32) if "dt_h" in taps else None
    y_h = scr("y_h", [D, L], BF16)
    x1_h = scr("x1_h", [L, D], F32)
    h2_h = scr("h2_h", [L, D], BF16)
    outp_h = scr("outp_h", [L, D], F32)
    slots_h = scr("slots_h", [L, E], I32) if "slots_h" in taps else None
    xinb_h = scr("xinb_h", [E * CAP, D + 16], BF16)
    xinT_h = scr("xinT_h", [D, CAP], BF16) if "xinT_h" in taps else None
    he_h = scr("he_h", [DI, CAP], BF16) if "he_h" in taps else None
    xmlp_h = scr("xmlp_h", [E * CAP + 1, D], BF16)

    p = P(nc, ctx)

    def sb(name, shape, dt):
        return ctx.enter_context(nc.sbuf_tensor(name, shape, dt))

    def ps(name, shape, dt=F32):
        return ctx.enter_context(nc.psum_tensor(name, shape, dt))

    sLD = p.sem("sLD")
    sLD2 = p.sem("sLD2")
    sST = p.sem("sST")
    sPE = p.sem("sPE")
    sACT = p.sem("sACT")
    sDVE = p.sem("sDVE")
    sPOOL = p.sem("sPOOL")
    sXl = [p.sem(f"sXl{i}") for i in range(2)]
    sXNT = p.sem("sXNT")
    sR1 = p.sem("sR1")
    sUC = [p.sem(f"sUC{i}") for i in range(2)]
    sXHG = p.sem("sXHG")
    sYSP = p.sem("sYSP")
    sYG = [p.sem(f"sYG{i}") for i in range(2)]
    sH2T = p.sem("sH2T")
    sGW = [p.sem(f"sGW{i}") for i in range(2)]
    sUW = [p.sem(f"sUW{i}") for i in range(2)]
    sDW = [p.sem(f"sDW{i}") for i in range(2)]
    sRES = [p.sem(f"sRES{i}") for i in range(2)]
    sCOMB = [p.sem(f"sCOMB{i}") for i in range(2)]
    sSCAT = [p.sem(f"sSCAT{i}") for i in range(2)]
    sXINT = p.sem("sXINT")
    sW12 = p.sem("sW12")
    sOPF = [p.sem(f"sOPF{i}") for i in range(2)]
    sGAT = p.sem("sGAT")
    sSTO = [p.sem(f"sSTO{i}") for i in range(2)]
    sFIN = p.sem("sFIN")
    ALLSEMS = [sLD, sLD2, sST, sPE, sACT, sDVE, sPOOL, sXNT, sR1, sXHG, sYSP,
               sH2T, sXINT, sW12, sGAT] + sXl + sUC + sYG + sGW + sUW + sDW +               sRES + sCOMB + sSCAT + sOPF + sSTO

    scope = {"es": None}

    def new_phase():
        snap = [(s, s.n) for s in ALLSEMS if s.n > 0]
        for en in ENGINES:
            p.emit(en, None, waits=snap, selfsync=False)
        if scope["es"] is not None:
            scope["es"].close()
        scope["es"] = ExitStack()

    def sbl(name, shape, dt):
        return scope["es"].enter_context(nc.sbuf_tensor(name, shape, dt))

    ident_b = sb("ident_b", [128, 128], BF16)
    ident_f = sb("ident_f", [128, 128], F32)
    wbctx = ExitStack()
    onehot_b = wbctx.enter_context(nc.sbuf_tensor("onehot_b", [128, DS * 128], BF16, side="right"))
    LT_f = sb("LT_f", [128, 128], F32)
    ones_f = sb("ones_f", [128, 1], F32)
    onesrow_f = sb("onesrow_f", [1, 128], F32)
    A_neg = sb("A_neg", [128, DS * NG], F32)
    dwk_sb = sb("dwk_sb", [128, KC * NG], F32)
    dwb_sb = sb("dwb_sb", [128, NG], F32)
    pwb_sb = sb("pwb_sb", [128, NG], F32)
    n1w_sb = sb("n1w_sb", [128, NG], F32)
    dtb_sb = sb("dtb_sb", [DH, 1], F32)
    choice_sb = sb("choice_sb", [128, E * NG], F32)
    xpwT_b = sb("xpwT_b", [128, 160], BF16)
    xpw_dt_f = sb("xpw_dt_f", [DH, DH], F32)
    dtwT_f = sb("dtwT_f", [DH, DH], F32)
    wcombT_b = sb("wcombT_b", [128, DH], BF16)
    opb_rep = sb("opb_rep", [128, D], F32)
    n2w_rep = sb("n2w_rep", [128, D], F32)
    ecap_row = sb("ecap_row", [1, E], F32)
    fin_t = sb("fin_t", [1, 1], F32)
    pw_wb = [wbctx.enter_context(nc.sbuf_tensor(f"pw_wb{g}", [128, D], BF16, side="right"))
             for g in range(NG)]

    PSW = [ps(f"psw{i}", [128, 1024], F32) for i in range(3)]
    PS = [PSW[0][:, 0:512], PSW[0][:, 512:1024], PSW[1][:, 0:512],
          PSW[1][:, 512:1024], PSW[2][:, 0:512], PSW[2][:, 512:1024]]

    p.dma("gpsimd", ident_b[:], c_ident, sLD2)
    p.dma("gpsimd", onehot_b[:], c_onehot, sLD2)
    p.dma("gpsimd", xpwT_b[0:64, :], xpwT, sLD2)
    p.dma("gpsimd", xpwT_b[64:128, :], xpwT, sLD2)
    for g in range(NG):
        p.dma("gpsimd", pw_wb[g][:], pw_wT[128 * g:128 * (g + 1), :], sLD2)
    p.dma("sync", ident_f[:], c_ident, sLD)
    p.dma("sync", LT_f[:], c_LT, sLD)
    p.dma("sync", ones_f[:], c_ones, sLD)
    p.dma("sync", onesrow_f[:], c_ones.rearrange("p o -> o p"), sLD)
    for g in range(NG):
        rs = slice(128 * g, 128 * (g + 1))
        p.dma("sync", dwk_sb[:, KC * g:KC * (g + 1)], dwk[rs, :], sLD)
        p.dma("sync", dwb_sb[:, g:g + 1], dw_bias[rs, :], sLD)
        p.dma("sync", pwb_sb[:, g:g + 1], pw_b[rs, :], sLD)
        p.dma("sync", n1w_sb[:, g:g + 1], n1w[rs, :], sLD)
        p.dma("sync", choice_sb[:, E * g:E * (g + 1)], choice_wT[rs, :], sLD)
    p.dma("sync", dtb_sb[:], dt_b, sLD)
    p.dma("sync", xpw_dt_f[:], xpw_dt, sLD)
    p.dma("sync", dtwT_f[:], dtwT, sLD)
    p.dma("sync", ecap_row[:], c_ecap, sLD)
    for g in range(NG):
        for hh in range(2):
            p.dma("sync", A_neg[64 * hh:64 * (hh + 1), DS * g:DS * (g + 1)],
                  A_log_t[2 * g + hh:2 * g + hh + 1, :].broadcast_to([64, DS]), sLD)
    p.dma("sync", opb_rep[0:1, :], opb, sLD)
    p.dma("sync", n2w_rep[0:1, :], n2w, sLD)

    p.emit("scalar", lambda e: e.activation(out=A_neg[:], in_=A_neg[:], func=AF.Exp),
           waits=[sLD.w(), sLD2.w()])
    p.emit("scalar", lambda e: e.mul(A_neg[:], A_neg[:], -1.0), inc=sACT)

    for dst in (opb_rep, n2w_rep):
        for hc in range(2):
            sl = slice(512 * hc, 512 * (hc + 1))
            pb = PS[hc]
            p.emit("tensor", lambda e, d=dst, sl=sl, pb=pb: e.matmul(
                pb[:], onesrow_f[:], d[0:1, sl], start=True, stop=True),
                waits=[sLD.w(), sDVE.w()], inc=sPE)
            p.emit("vector", lambda e, d=dst, sl=sl, pb=pb: e.tensor_copy(
                out=d[:, sl], in_=pb[:]), waits=[sPE.w()], inc=sDVE)

    bank6 = ps("bank6", [128, 512], F32)
    bank7 = ps("bank7", [128, 512], F32)
    bank67 = [bank6, bank7]
    ps_wc = bank7[0:64, 0:64]
    p.emit("tensor", lambda e: e.matmul(ps_wc[:], xpw_dt_f[:], dtwT_f[:],
                                        start=True, stop=True),
           waits=[sLD.w(), sDVE.w()], inc=sPE)
    p.emit("vector", lambda e: e.tensor_copy(out=wcombT_b[0:64, :], in_=ps_wc[:]),
           waits=[sPE.w()], inc=sDVE)
    p.emit("vector", lambda e: e.tensor_copy(out=wcombT_b[64:128, :], in_=ps_wc[:]),
           inc=sDVE)

    c_ss = sb("c_ss", [128, 1], F32)
    c_sum = sb("c_sum", [128, 1], F32)
    c_mu = sb("c_mu", [128, 1], F32)
    c_var = sb("c_var", [128, 1], F32)
    c_t1 = sb("c_t1", [128, 1], F32)
    c_rstd = sb("c_rstd", [128, 1], F32)
    xsq = sb("xsq", [128, D], F32)
    probs_sb = sb("probs_sb", [128, NT128 * E], F32)
    slots_i = sb("slots_i", [128, NT128 * E], I32)
    slots_g = sb("slots_g", [128, NT128 * E], I32)
    ps_rt = bank7[:, 64:64 + E]
    new_phase()
    xt = [sbl(f"xt{i}", [128, D], F32) for i in range(2)]
    xnt2p = [sbl(f"xnt2p{i}", [128, D], F32) for i in range(2)]
    xnb2 = [sbl(f"xnb2_{i}", [128, D], BF16) for i in range(2)]
    xs_full = sbl("xs_full", [128, 1024], F32)
    r_mx = sbl("r_mx", [128, 1], F32)
    r_se = sbl("r_se", [128, 1], F32)
    r_ex = sbl("r_ex", [128, E], F32)
    c_ss2 = [sbl(f"c_ss2_{i}", [128, 1], F32) for i in range(2)]
    c_mu2 = [sbl(f"c_mu2_{i}", [128, 1], F32) for i in range(2)]
    c_var2 = [sbl(f"c_var2_{i}", [128, 1], F32) for i in range(2)]
    c_rstd2 = [sbl(f"c_rstd2_{i}", [128, 1], F32) for i in range(2)]
    c_t12 = [sbl(f"c_t12_{i}", [128, 1], F32) for i in range(2)]

    xload_done = {}
    act_probs = {}
    dve_xnb = {}
    dve_ct1 = {}
    for i in range(NT128):
        par = i % 2
        buf = xt[par]
        xn_c = xnt2p[par]
        cm, cv, cr, ct = c_mu2[par], c_var2[par], c_rstd2[par], c_t12[par]
        wts = [xload_done[i - 2]] if i >= 2 else []
        p.dma("sync", buf[:], x_in[128 * i:128 * (i + 1), :], sXl[par], waits=wts)
        nld = sXl[par].w()
        wsq = [nld]
        if i >= 2:
            wsq.append(dve_ct1[i - 2])
        p.emit("scalar", lambda e, b=buf, par=par: e.activation(
            out=xsq[:], in_=b[:], func=AF.Square, accum_out=c_ss2[par][:, 0:1]),
            waits=wsq, inc=sACT)
        nsq = sACT.w()
        p.emit("vector", lambda e, b=buf, cm=cm: e.reduce_sum(out=cm[:], in_=b[:],
                                                              axis=AX.X),
               waits=[nld, nsq])
        p.emit("vector", lambda e, cm=cm: e.tensor_single_scalar(
            out=cm[:], in_=cm[:], scalar=1.0 / D, op=ALU.mult))
        p.emit("vector", lambda e, par=par, ct=ct: e.tensor_single_scalar(
            out=ct[:], in_=c_ss2[par][:], scalar=1.0 / D, op=ALU.mult), inc=sDVE)
        dve_ct1[i] = sDVE.w()
        p.emit("vector", lambda e, cm=cm, cv=cv: e.tensor_mul(cv[:], cm[:], cm[:]))
        p.emit("vector", lambda e, cv=cv, ct=ct: e.tensor_sub(cv[:], ct[:], cv[:]))
        p.emit("vector", lambda e, cv=cv, ct=ct: e.tensor_single_scalar(
            out=ct[:], in_=cv[:], scalar=1e-5, op=ALU.add))
        p.emit("vector", lambda e, cr=cr, ct=ct: e.reciprocal(cr[:], ct[:]), inc=sDVE)
        p.emit("scalar", lambda e, cr=cr: e.activation(out=cr[:], in_=cr[:],
                                                       func=AF.Sqrt),
               waits=[sDVE.w()], inc=sACT)
        p.emit("vector", lambda e, cm=cm: e.tensor_single_scalar(
            out=cm[:], in_=cm[:], scalar=-1.0, op=ALU.mult), waits=[sACT.w()])
        p.emit("vector", lambda e, b=buf, cm=cm, cr=cr, xn_c=xn_c: e.scalar_tensor_tensor(
            out=xn_c[:], in0=b[:], scalar=cm[:, 0:1],
            in1=cr[:].broadcast_to([128, D]), op0=ALU.add, op1=ALU.mult),
            waits=[sST.w()])
        p.emit("vector", lambda e, cr=cr, ct=ct: e.tensor_mul(ct[:], cr[:], cr[:]))
        p.emit("vector", lambda e, cv=cv, ct=ct: e.tensor_mul(ct[:], ct[:], cv[:]))
        p.emit("vector", lambda e, ct=ct: e.tensor_single_scalar(
            out=ct[:], in_=ct[:], scalar=1e-6, op=ALU.add))
        p.emit("vector", lambda e, ct=ct: e.reciprocal(ct[:], ct[:]), inc=sDVE)
        p.emit("scalar", lambda e, ct=ct: e.activation(out=ct[:], in_=ct[:],
                                                       func=AF.Sqrt),
               waits=[sDVE.w()], inc=sACT)
        p.emit("vector", lambda e, par=par, ct=ct, xn_c=xn_c: e.tensor_scalar_mul(
            xnb2[par][:], xn_c[:], ct[:, 0:1]), waits=[sACT.w()], inc=sDVE)
        dve_xnb[i] = sDVE.w()
        xload_done[i] = sDVE.w()
        p.dma("sync", xn_f32[128 * i:128 * (i + 1), :], xn_c[:], sST,
              waits=[dve_xnb[i]])
        p.dma("sync", xn_b16[128 * i:128 * (i + 1), :], xnb2[par][:], sST)
        for j in range(NG):
            pt = PS[4 + (j // 4)]
            p.emit("tensor", lambda e, j=j, pt=pt, xn_c=xn_c: e.transpose(
                pt[:, 128 * (j % 4):128 * (j % 4) + 128],
                xn_c[:, 128 * j:128 * (j + 1)], ident_f[:]),
                waits=([dve_xnb[i], sACT.w()] if j == 0 else []), inc=sPE)
            if j % 4 == 3:
                p.emit("scalar", lambda e, j=j: e.activation(
                    out=xs_full[:, 512 * (j // 4):512 * (j // 4) + 512],
                    in_=PS[4 + (j // 4)][:], func=AF.Copy),
                    waits=[sPE.w()], inc=sACT)
        nxs = sACT.w()
        for j in range(NG):
            p.emit("tensor", lambda e, j=j: e.matmul(
                ps_rt[:], xs_full[:, 128 * j:128 * (j + 1)],
                choice_sb[:, E * j:E * (j + 1)],
                start=(j == 0), stop=(j == NG - 1)),
                waits=([nxs] if j == 0 else []),
                inc=(sPE if j == NG - 1 else None))
        p.emit("vector", lambda e: e.reduce_max(out=r_mx[:], in_=ps_rt[:], axis=AX.X),
               waits=[sPE.w()])
        p.emit("vector", lambda e: e.tensor_single_scalar(out=r_mx[:], in_=r_mx[:],
                                                          scalar=-1.0, op=ALU.mult),
               inc=sDVE)
        p.emit("scalar", lambda e: e.activation(out=r_ex[:], in_=ps_rt[:], func=AF.Exp,
                                                bias=r_mx[:, 0:1], accum_out=r_se[:, 0:1]),
               waits=[sDVE.w()], inc=sACT)
        p.emit("vector", lambda e: e.reciprocal(r_se[:], r_se[:]), waits=[sACT.w()])
        p.emit("vector", lambda e, i=i: e.tensor_scalar_mul(
            probs_sb[:, E * i:E * (i + 1)], r_ex[:], r_se[:, 0:1]), inc=sDVE)
        p.dma("sync", probs_h[128 * i:128 * (i + 1), :],
              probs_sb[:, E * i:E * (i + 1)], sST, waits=[sDVE.w()])
    if nphase <= 1:
        p.run_block()
        scope["es"].close()
        ctx.close()
        return nc

    new_phase()
    xnT_g = sbl("xnT_g", [128, L], BF16)
    u_g = sbl("u_g", [128, KC - 1 + L], F32)
    cacc = [sbl(f"cacc{i}", [128, L], F32) for i in range(2)]
    ucb = sbl("ucb", [128, L], BF16)

    p.emit("gpsimd", lambda e: e.memset(u_g[:, 0:KC - 1], 0.0), inc=sPOOL)
    for g in range(NG):
        p.dma("sync", xnT_g[:], xn_b16[:, 128 * g:128 * (g + 1)], sXNT,
              waits=[sST.w(), sDVE.w()], transpose=True)
        p.emit("vector", lambda e, g=g: e.tensor_single_scalar(
            out=u_g[:, KC - 1:], in_=xnT_g[:], scalar=n1w_sb[:, g:g + 1],
            op=ALU.mult), waits=[sXNT.w(), sPOOL.w()])
        p.emit("vector", lambda e, g=g: e.tensor_single_scalar(
            out=cacc[0][:], in_=u_g[:, 3:3 + L], scalar=dwk_sb[:, KC * g + 3:KC * g + 4],
            op=ALU.mult))
        p.emit("vector", lambda e, g=g: e.scalar_tensor_tensor(
            out=cacc[1][:], in0=u_g[:, 2:2 + L], scalar=dwk_sb[:, KC * g + 2:KC * g + 3],
            in1=cacc[0][:], op0=ALU.mult, op1=ALU.add))
        p.emit("vector", lambda e, g=g: e.scalar_tensor_tensor(
            out=cacc[0][:], in0=u_g[:, 1:1 + L], scalar=dwk_sb[:, KC * g + 1:KC * g + 2],
            in1=cacc[1][:], op0=ALU.mult, op1=ALU.add))
        p.emit("vector", lambda e, g=g: e.scalar_tensor_tensor(
            out=cacc[1][:], in0=u_g[:, 0:0 + L], scalar=dwk_sb[:, KC * g + 0:KC * g + 1],
            in1=cacc[0][:], op0=ALU.mult, op1=ALU.add))
        p.emit("vector", lambda e, g=g: e.tensor_single_scalar(
            out=ucb[:], in_=cacc[1][:], scalar=dwb_sb[:, g:g + 1], op=ALU.add),
            waits=[sST.w()], inc=sDVE)
        p.dma("sync", uc_h[128 * g:128 * (g + 1), :], ucb[:], sST, waits=[sDVE.w()])
    if nphase <= 2:
        p.run_block()
        scope["es"].close()
        ctx.close()
        return nc

    new_phase()
    ucg = [[sbl(f"ucg{par}_{g}", [128, 512], BF16) for g in range(NG)] for par in (0, 1)]
    xhb_t = sbl("xhb_t", [128, 512], BF16)
    uc_done = {}
    for tc in range(NT512):
        par = tc % 2
        wts = [uc_done[tc - 2]] if tc >= 2 else [sST.w()]
        for g in range(NG):
            p.dma("sync", ucg[par][g][:],
                  uc_h[128 * g:128 * (g + 1), 512 * tc:512 * (tc + 1)], sUC[par],
                  waits=(wts if g == 0 else []))
        nld = sUC[par].w()
        for mc in range(NG):
            pb = PS[mc % 2]
            for g in range(NG):
                p.emit("tensor", lambda e, mc=mc, g=g, pb=pb, par=par: e.matmul(
                    pb[:], pw_wb[g][:, 128 * mc:128 * (mc + 1)], ucg[par][g][:],
                    start=(g == 0), stop=(g == NG - 1)),
                    waits=([nld, sACT.w()] if g == 0 else []),
                    inc=(sPE if g == NG - 1 else None))
            p.emit("scalar", lambda e, mc=mc, pb=pb: e.activation(
                out=xhb_t[:], in_=pb[:], func=AF.Identity, bias=pwb_sb[:, mc:mc + 1]),
                waits=[sPE.w(), sST.w()], inc=sACT)
            p.dma("sync", xh_h[128 * mc:128 * (mc + 1), 512 * tc:512 * (tc + 1)],
                  xhb_t[:], sST, waits=[sACT.w()])
        uc_done[tc] = sPE.w()
    if nphase <= 3:
        p.run_block()
        scope["es"].close()
        ctx.close()
        return nc

    new_phase()
    xhg = sbl("xhg", [128, L], BF16)
    dtT = sbl("dtT", [128, L], F32)
    Bt = sbl("Bt", [128, L], BF16)
    Ct = sbl("Ct", [128, L], BF16)
    du = sbl("du", [128, L], BF16)
    t_abs_t = sbl("t_abs_t", [128, 512], F32)
    t_ln_t = sbl("t_ln_t", [128, 512], F32)
    t_abs2 = [t_abs_t[0:64, :], t_abs_t[64:128, :]]
    t_ln2 = [t_ln_t[0:64, :], t_ln_t[64:128, :]]
    abar = [sbl(f"abar{i}", [128, 1024], F32) for i in range(2)]
    hsc = [sbl(f"hsc{i}", [128, 1024], F32) for i in range(2)]
    wsc_f = sbl("wsc_f", [128, 1024], F32)
    zsc2 = [sbl(f"zsc2_{i}", [128, 1024], BF16) for i in range(2)]
    hlast = sbl("hlast", [128, DS], F32)
    ybf = [sbl(f"ybf{i}", [128, 1024], BF16) for i in range(2)]
    NCH = L // 1024

    for g in range(NG):
        p.dma("sync", xhg[:], xh_h[128 * g:128 * (g + 1), :], sXHG,
              waits=[sDVE.w(), sPE.w(), sST.w()])
        p.emit("gpsimd", lambda e: e.memset(Bt[:], 0.0), waits=[sPE.w()], inc=sPOOL)
        p.emit("gpsimd", lambda e: e.memset(Ct[:], 0.0), inc=sPOOL)
        nld = sXHG.w()
        npool = sPOOL.w()
        act_evp = {}
        dve_addp = {}
        for hh in range(2):
            rb = slice(64 * hh, 64 * hh + 64)
            ob = 64 * hh
            for tc in range(NT512):
                htc = hh * NT512 + tc
                par2 = htc % 2
                sl = slice(512 * tc, 512 * (tc + 1))
                pdt, pB, pC = PS[par2], PS[2 + par2], PS[4 + par2]
                ta, tl = t_abs2[par2], t_ln2[par2]
                wmm = [act_evp[htc - 2]] if htc >= 2 else [nld, npool]
                p.emit("tensor", lambda e, rb=rb, sl=sl, pdt=pdt: e.matmul(
                    pdt[0:64, 0:512], wcombT_b[rb, :], xhg[rb, sl], start=True, stop=True),
                    waits=wmm, inc=sPE)
                p.emit("tensor", lambda e, rb=rb, sl=sl, pB=pB: e.matmul(
                    pB[0:48, 0:512], xpwT_b[rb, 64:112], xhg[rb, sl], start=True, stop=True),
                    inc=sPE)
                p.emit("tensor", lambda e, rb=rb, sl=sl, pC=pC: e.matmul(
                    pC[0:48, 0:512], xpwT_b[rb, 112:160], xhg[rb, sl], start=True, stop=True),
                    inc=sPE)
                npe = sPE.w()
                wsp = [npe]
                if htc >= 2:
                    wsp.append(dve_addp[htc - 2])
                p.emit("scalar", lambda e, pdt=pdt, ta=ta: e.activation(
                    out=ta[:], in_=pdt[0:64, 0:512], func=AF.Abs, bias=dtb_sb[:, 0:1]),
                    waits=wsp)
                p.emit("scalar", lambda e, ta=ta, tl=tl: e.activation(
                    out=tl[:], in_=ta[:], func=AF.Exp, scale=-1.0))
                p.emit("scalar", lambda e, tl=tl: e.activation(
                    out=tl[:], in_=tl[:], func=AF.Ln, bias=1.0))
                p.emit("scalar", lambda e, pdt=pdt, ta=ta: e.activation(
                    out=ta[:], in_=pdt[0:64, 0:512], func=AF.Relu, bias=dtb_sb[:, 0:1]),
                    inc=sACT)
                nsp = sACT.w()
                p.emit("scalar", lambda e, pB=pB, ob=ob, sl=sl: e.activation(
                    out=Bt[ob:ob + 48, sl], in_=pB[0:48, 0:512], func=AF.Copy))
                p.emit("scalar", lambda e, pC=pC, ob=ob, sl=sl: e.activation(
                    out=Ct[ob:ob + 48, sl], in_=pC[0:48, 0:512], func=AF.Copy), inc=sACT)
                act_evp[htc] = sACT.w()
                p.emit("vector", lambda e, ob=ob, sl=sl, ta=ta, tl=tl: e.tensor_add(
                    dtT[ob:ob + 64, sl], ta[:], tl[:]), waits=[nsp], inc=sDVE)
                dve_addp[htc] = sDVE.w()
        if dt_hD is not None:
            p.dma("gpsimd", dt_hD[128 * g:128 * (g + 1), :], dtT[:], sLD2,
                  waits=[sACT.w(), sDVE.w()])
        p.emit("vector", lambda e: e.tensor_mul(du[:], dtT[:], xhg[:]),
               waits=[sACT.w()], inc=sDVE)

        nprj_act = sACT.w()
        nprj_dve = sDVE.w()
        psB, psC, psY = PSW[0], PSW[1], PSW[2]
        dve_after_w = {}
        dve_after_tts = {}
        dve_after_z = {}
        pe_after_B = {}
        pe_after_C = {}
        pe_after_Y = {}
        act_after_ab = {}
        act_after_ev = {}
        pool_hl = {}
        yst = {}

        def emit_yacc(j):
            chj, sj = divmod(j, DS)
            w0 = [dve_after_z[j]]
            if sj == 0 and chj > 0:
                w0.append(act_after_ev[chj - 1])
            for half in range(2):
                p.emit("tensor", lambda e, j=j, half=half: e.matmul(
                    psY[:, 512 * half:512 * (half + 1)], ident_b[:],
                    zsc2[j % 2][:, 512 * half:512 * (half + 1)],
                    start=(sj == 0), stop=(sj == DS - 1)),
                    waits=(w0 if half == 0 else []), inc=sPE)
            pe_after_Y[j] = sPE.w()
            if sj == DS - 1:
                yb = ybf[chj % 2]
                wv = [pe_after_Y[j]]
                if chj >= 2:
                    wv.append(yst[chj - 2])
                p.emit("scalar", lambda e, yb=yb: e.activation(
                    out=yb[:], in_=psY[:], func=AF.Copy), waits=wv, inc=sACT)
                act_after_ev[chj] = sACT.w()
                p.dma("sync", y_h[128 * g:128 * (g + 1), 1024 * chj:1024 * (chj + 1)],
                      yb[:], sYSP, waits=[act_after_ev[chj]])
                yst[chj] = sYSP.w()

        K = NCH * DS
        for ch in range(NCH):
            chsl = slice(1024 * ch, 1024 * (ch + 1))
            for s in range(DS):
                k = ch * DS + s
                par = k % 2
                oh = onehot_b[:, 128 * s:128 * (s + 1)]
                ab = abar[par]
                hc = hsc[par]
                wB = [dve_after_w[k - 1]] if k >= 1 else [nprj_dve, nprj_act]
                for half in range(2):
                    sl5 = slice(1024 * ch + 512 * half, 1024 * ch + 512 * (half + 1))
                    p.emit("tensor", lambda e, oh=oh, sl5=sl5, half=half: e.matmul(
                        psB[:, 512 * half:512 * (half + 1)], oh, Bt[:, sl5],
                        start=True, stop=True),
                        waits=(wB if half == 0 else []), inc=sPE)
                pe_after_B[k] = sPE.w()
                wC = [dve_after_z[k - 1]] if k >= 1 else []
                for half in range(2):
                    sl5 = slice(1024 * ch + 512 * half, 1024 * ch + 512 * (half + 1))
                    p.emit("tensor", lambda e, oh=oh, sl5=sl5, half=half: e.matmul(
                        psC[:, 512 * half:512 * (half + 1)], oh, Ct[:, sl5],
                        start=True, stop=True),
                        waits=(wC if half == 0 else []), inc=sPE)
                pe_after_C[k] = sPE.w()
                if k >= 1:
                    emit_yacc(k - 1)
                wA = [dve_after_tts[k - 2]] if k >= 2 else [nprj_dve]
                p.emit("scalar", lambda e, s=s, g=g, chsl=chsl, ab=ab: e.activation(
                    out=ab[:], in_=dtT[:, chsl], func=AF.Exp,
                    scale=A_neg[:, DS * g + s:DS * g + s + 1]), waits=wA, inc=sACT)
                act_after_ab[k] = sACT.w()
                p.emit("vector", lambda e, chsl=chsl: e.scalar_tensor_tensor(
                    out=wsc_f[:], in0=du[:, chsl], scalar=1.0,
                    in1=psB[:], op0=ALU.mult, op1=ALU.mult),
                    waits=[pe_after_B[k]], inc=sDVE)
                dve_after_w[k] = sDVE.w()
                init = 0.0 if ch == 0 else hlast[:, s:s + 1]
                wtts = [act_after_ab[k]]
                if k >= DS:
                    wtts.append(pool_hl[k - DS])
                p.emit("vector", lambda e, ab=ab, hc=hc, init=init: e.tensor_tensor_scan(
                    out=hc[:], data0=ab[:], data1=wsc_f[:], initial=init,
                    op0=ALU.mult, op1=ALU.add), waits=wtts, inc=sDVE)
                dve_after_tts[k] = sDVE.w()
                p.emit("gpsimd", lambda e, hc=hc, s=s: e.tensor_copy(
                    out=hlast[:, s:s + 1], in_=hc[:, 1023:1024]),
                    waits=[dve_after_tts[k]], inc=sPOOL)
                pool_hl[k] = sPOOL.w()
                wz = [pe_after_C[k]]
                if k >= 2:
                    wz.append(pe_after_Y[k - 2])
                p.emit("vector", lambda e, hc=hc, k=k: e.scalar_tensor_tensor(
                    out=zsc2[k % 2][:], in0=hc[:], scalar=1.0,
                    in1=psC[:], op0=ALU.mult, op1=ALU.mult),
                    waits=wz, inc=sDVE)
                dve_after_z[k] = sDVE.w()
        emit_yacc(K - 1)
    if nphase <= 4:
        p.run_block()
        scope["es"].close()
        ctx.close()
        return nc

    new_phase()
    op_wb = pw_wb
    for g in range(NG):
        p.dma("gpsimd", op_wb[g][:], out_projT[128 * g:128 * (g + 1), :], sLD2,
              waits=([sPE.w()] if g == 0 else []))
    yg = [[sbl(f"yg{par}_{g}", [128, 128], BF16) for g in range(NG)] for par in (0, 1)]
    x1t = [sbl(f"x1t{i}", [128, D], F32) for i in range(2)]
    h2b2 = [sbl(f"h2b{i}", [128, D], BF16) for i in range(2)]
    xnt2 = [sbl(f"xnt2_{i}", [128, D], F32) for i in range(2)]
    cs5 = [sbl(f"cs5_{i}", [128, 1], F32) for i in range(2)]
    ct5 = [sbl(f"ct5_{i}", [128, 1], F32) for i in range(2)]
    st5 = {}
    th_lo = sbl("th_lo", [1, E], F32)
    th_hi = sbl("th_hi", [1, E], F32)
    th_mid = sbl("th_mid", [1, E], F32)
    cnt_row = sbl("cnt_row", [1, E], F32)
    gtk = sbl("gtk", [1, E], F32)
    tmp2_e = sbl("tmp2_e", [1, E], F32)
    threp = sbl("threp", [128, E], F32)
    lorep = sbl("lorep", [128, E], F32)
    msk = sbl("msk", [128, NT128 * E], F32)
    msk2 = sbl("msk2", [128, NT128 * E], F32)
    sel = sbl("sel", [128, NT128 * E], F32)
    carry = sbl("carry", [1, E], F32)
    carryb = sbl("carryb", [128, E], F32)
    rrep = sbl("rrep", [128, E], F32)
    ecapb = sbl("ecapb", [128, E], F32)
    slots_f = sbl("slots_f", [128, NT128 * E], F32)
    tmp_e = sbl("tmp_e", [128, E], F32)
    tmp2g = sbl("tmp2g", [128, E], F32)
    tmp3g = sbl("tmp3g", [128, E], F32)
    ps_cnt = bank6[0:1, 0:NT128 * E]
    ps_rnk = bank7[:, 0:E]
    ps_tot = bank7[0:1, E:2 * E]
    ps_brd = bank7[:, 2 * E:3 * E]

    def brd_cols(src_):
        return src_.unsqueeze(1).broadcast_to([128, NT128, E])

    p7a = {}

    def p7a_part1():
        p.emit("gpsimd", lambda e: e.memset(th_lo[:], 0.0), waits=[sDVE.w()], inc=sPOOL)
        p.emit("gpsimd", lambda e: e.memset(th_hi[:], 1.0), inc=sPOOL)
        first = [sPOOL.w()]
        yield
        NBIS = 46
        for itb in range(NBIS):
            p.emit("vector", lambda e: e.tensor_add(th_mid[:], th_lo[:], th_hi[:]),
                   waits=first)
            first = []
            p.emit("vector", lambda e: e.tensor_single_scalar(
                out=th_mid[:], in_=th_mid[:], scalar=0.5, op=ALU.mult), inc=sDVE)
            p.emit("tensor", lambda e: e.matmul(ps_brd[:], onesrow_f[:], th_mid[:],
                                                start=True, stop=True),
                   waits=[sDVE.w()], inc=sPE)
            p.emit("vector", lambda e: e.tensor_copy(out=threp[:], in_=ps_brd[:]),
                   waits=[sPE.w()], inc=sDVE)
            p.emit("vector", lambda e: e.tensor_tensor(
                out=msk[:].rearrange("p (o e) -> p o e", e=E),
                in0=probs_sb[:].rearrange("p (o e) -> p o e", e=E),
                in1=brd_cols(threp[:]), op=ALU.is_gt), inc=sDVE)
            p.emit("tensor", lambda e: e.matmul(ps_cnt[:], ones_f[:], msk[:],
                                                start=True, stop=True),
                   waits=[sDVE.w()], inc=sPE)
            p.emit("vector", lambda e: e.reduce_sum(
                out=cnt_row[:], in_=ps_cnt[0:1, :].rearrange("o (i e) -> o e i", e=E),
                axis=AX.X), waits=[sPE.w()])
            p.emit("vector", lambda e: e.tensor_single_scalar(
                out=gtk[:], in_=cnt_row[:], scalar=float(CAP), op=ALU.is_gt))
            p.emit("vector", lambda e: e.tensor_sub(tmp2_e[:], th_mid[:], th_lo[:]))
            p.emit("vector", lambda e: e.tensor_mul(tmp2_e[:], tmp2_e[:], gtk[:]))
            p.emit("vector", lambda e: e.tensor_add(th_lo[:], th_lo[:], tmp2_e[:]))
            p.emit("vector", lambda e: e.tensor_sub(tmp2_e[:], th_hi[:], th_mid[:]))
            p.emit("vector", lambda e: e.tensor_mul(tmp2_e[:], tmp2_e[:], gtk[:]))
            p.emit("vector", lambda e: e.tensor_add(th_hi[:], th_mid[:], tmp2_e[:]),
                   inc=sDVE)
            yield

        p.emit("tensor", lambda e: e.matmul(ps_brd[:], onesrow_f[:], th_hi[:],
                                            start=True, stop=True),
               waits=[sDVE.w()], inc=sPE)
        p.emit("vector", lambda e: e.tensor_copy(out=threp[:], in_=ps_brd[:]),
               waits=[sPE.w()], inc=sDVE)
        yield
        p.emit("tensor", lambda e: e.matmul(ps_brd[:], onesrow_f[:], th_lo[:],
                                            start=True, stop=True),
               waits=[sDVE.w()], inc=sPE)
        p.emit("vector", lambda e: e.tensor_copy(out=lorep[:], in_=ps_brd[:]),
               waits=[sPE.w()], inc=sDVE)
        yield
        p.emit("vector", lambda e: e.tensor_tensor(
            out=msk[:].rearrange("p (o e) -> p o e", e=E),
            in0=probs_sb[:].rearrange("p (o e) -> p o e", e=E),
            in1=brd_cols(threp[:]), op=ALU.is_gt))
        p.emit("vector", lambda e: e.tensor_tensor(
            out=msk2[:].rearrange("p (o e) -> p o e", e=E),
            in0=probs_sb[:].rearrange("p (o e) -> p o e", e=E),
            in1=brd_cols(lorep[:]), op=ALU.is_gt))
        p.emit("vector", lambda e: e.tensor_sub(msk2[:], msk2[:], msk[:]), inc=sDVE)
        yield
        p.emit("tensor", lambda e: e.matmul(ps_cnt[:], ones_f[:], msk[:],
                                            start=True, stop=True),
               waits=[sDVE.w()], inc=sPE)
        p.emit("vector", lambda e: e.reduce_sum(
            out=cnt_row[:], in_=ps_cnt[0:1, :].rearrange("o (i e) -> o e i", e=E),
            axis=AX.X), waits=[sPE.w()])
        p.emit("vector", lambda e: e.tensor_single_scalar(
            out=cnt_row[:], in_=cnt_row[:], scalar=-1.0, op=ALU.mult))
        p.emit("vector", lambda e: e.tensor_single_scalar(
            out=cnt_row[:], in_=cnt_row[:], scalar=float(CAP), op=ALU.add), inc=sDVE)
        yield
        p.emit("tensor", lambda e: e.matmul(ps_brd[:], onesrow_f[:], cnt_row[:],
                                            start=True, stop=True),
               waits=[sDVE.w()], inc=sPE)
        p.emit("vector", lambda e: e.tensor_copy(out=rrep[:], in_=ps_brd[:]),
               waits=[sPE.w()], inc=sDVE)
        yield
        p.emit("tensor", lambda e: e.matmul(ps_brd[:], onesrow_f[:], ecap_row[:],
                                            start=True, stop=True),
               waits=[sDVE.w()], inc=sPE)
        p.emit("vector", lambda e: e.tensor_copy(out=ecapb[:], in_=ps_brd[:]),
               waits=[sPE.w()], inc=sDVE)
        yield
        p.emit("gpsimd", lambda e: e.memset(carry[:], 0.0), waits=[sDVE.w()], inc=sPOOL)
        npool_c = sPOOL.w()
        yield
        for i in range(NT128):
            sl = slice(E * i, E * (i + 1))
            p.emit("tensor", lambda e, sl=sl: e.matmul(
                ps_rnk[:], LT_f[:], msk2[:, sl], start=True, stop=True),
                waits=[npool_c, sDVE.w()], inc=sPE)
            p.emit("tensor", lambda e, sl=sl: e.matmul(
                ps_tot[:], ones_f[:], msk2[:, sl], start=True, stop=True), inc=sPE)
            p.emit("tensor", lambda e: e.matmul(
                ps_brd[:], onesrow_f[:], carry[:], start=True, stop=True), inc=sPE)
            p.emit("vector", lambda e: e.tensor_copy(out=carryb[:], in_=ps_brd[:]),
                   waits=[sPE.w()])
            p.emit("vector", lambda e, sl=sl: e.tensor_sub(tmp_e[:], ps_rnk[:],
                                                           msk2[:, sl]))
            p.emit("vector", lambda e: e.tensor_add(tmp_e[:], tmp_e[:], carryb[:]))
            p.emit("vector", lambda e: e.tensor_tensor(out=tmp_e[:], in0=tmp_e[:],
                                                       in1=rrep[:], op=ALU.is_lt))
            p.emit("vector", lambda e, sl=sl: e.tensor_mul(tmp_e[:], tmp_e[:],
                                                           msk2[:, sl]))
            p.emit("vector", lambda e, sl=sl: e.tensor_add(sel[:, sl], msk[:, sl],
                                                           tmp_e[:]))
            p.emit("vector", lambda e: e.tensor_add(carry[:], carry[:], ps_tot[0:1, :]),
                   inc=sDVE)
            yield
        p.emit("gpsimd", lambda e: e.memset(carry[:], 0.0), waits=[sDVE.w()], inc=sPOOL)
        npool_c2 = sPOOL.w()
        yield
        for i in range(NT128):
            sl = slice(E * i, E * (i + 1))
            p.emit("tensor", lambda e, sl=sl: e.matmul(
                ps_rnk[:], LT_f[:], sel[:, sl], start=True, stop=True),
                waits=[npool_c2, sDVE.w()], inc=sPE)
            p.emit("tensor", lambda e, sl=sl: e.matmul(
                ps_tot[:], ones_f[:], sel[:, sl], start=True, stop=True), inc=sPE)
            p.emit("tensor", lambda e: e.matmul(
                ps_brd[:], onesrow_f[:], carry[:], start=True, stop=True), inc=sPE)
            p.emit("vector", lambda e: e.tensor_copy(out=tmp_e[:], in_=ps_brd[:]),
                   waits=[sPE.w()])
            p.emit("vector", lambda e, sl=sl: e.tensor_sub(slots_f[:, sl], ps_rnk[:],
                                                           sel[:, sl]))
            p.emit("vector", lambda e, sl=sl: e.tensor_add(slots_f[:, sl],
                                                           slots_f[:, sl], tmp_e[:]))
            p.emit("vector", lambda e, sl=sl: e.tensor_add(slots_f[:, sl],
                                                           slots_f[:, sl], ecapb[:]))
            p.emit("vector", lambda e, sl=sl: e.tensor_single_scalar(
                out=tmp_e[:], in_=sel[:, sl], scalar=-1.0, op=ALU.mult))
            p.emit("vector", lambda e: e.tensor_single_scalar(
                out=tmp_e[:], in_=tmp_e[:], scalar=1.0, op=ALU.add))
            p.emit("vector", lambda e, sl=sl: e.tensor_mul(tmp2g[:], slots_f[:, sl],
                                                           sel[:, sl]))
            p.emit("vector", lambda e: e.tensor_single_scalar(
                out=tmp3g[:], in_=tmp_e[:], scalar=float(E * CAP), op=ALU.mult))
            p.emit("vector", lambda e: e.tensor_add(tmp2g[:], tmp2g[:], tmp3g[:]))
            p.emit("vector", lambda e, sl=sl: e.tensor_copy(out=slots_g[:, sl],
                                                            in_=tmp2g[:]))
            p.emit("vector", lambda e: e.tensor_single_scalar(
                out=tmp_e[:], in_=tmp_e[:], scalar=1.0e9, op=ALU.mult))
            p.emit("vector", lambda e, sl=sl: e.tensor_add(slots_f[:, sl],
                                                           slots_f[:, sl], tmp_e[:]))
            p.emit("vector", lambda e, sl=sl: e.tensor_copy(out=slots_i[:, sl],
                                                            in_=slots_f[:, sl]))
            p.emit("vector", lambda e: e.tensor_add(carry[:], carry[:], ps_tot[0:1, :]),
                   inc=sDVE)
            yield
        p7a["nslots"] = sDVE.w()

    gen1 = p7a_part1()

    def drv1(n=1):
        for _ in range(n):
            if next(gen1, "E") == "E":
                return

    nld2_w = sLD2.w()
    y_done = {}
    act_sq5 = {}
    dve_h2 = {}
    dve_x1 = {}
    for i in range(NT128):
        drv1(3)
        par = i % 2
        wts = [y_done[i - 2]] if i >= 2 else [nld2_w, sYSP.w()]
        for g in range(NG):
            p.dma("sync", yg[par][g][:],
                  y_h[128 * g:128 * (g + 1), 128 * i:128 * (i + 1)], sYG[par],
                  waits=(wts if g == 0 else []))
        p.dma("sync", xnt2[par][:], xn_f32[128 * i:128 * (i + 1), :], sYG[par])
        nld = sYG[par].w()
        pp = [PS[0], PS[1]] if par == 0 else [PS[2], PS[3]]
        wmm5 = [nld]
        if i >= 2:
            wmm5.append(dve_x1[i - 2])
        for half in range(2):
            for g in range(NG):
                p.emit("tensor", lambda e, g=g, half=half, pp=pp, par=par: e.matmul(
                    pp[half][:], yg[par][g][:], op_wb[g][:, 512 * half:512 * (half + 1)],
                    start=(g == 0), stop=(g == NG - 1)),
                    waits=(wmm5 if (half == 0 and g == 0) else []),
                    inc=(sPE if g == NG - 1 else None))
        npe = sPE.w()
        x1p = x1t[par]
        wadd = [npe]
        if i >= 2:
            wadd.append(st5[i - 2])
        p.emit("vector", lambda e, pp=pp, x1p=x1p, par=par: e.tensor_add(
            x1p[:, 0:512], pp[0][:], opb_rep[:, 0:512]), waits=wadd)
        p.emit("vector", lambda e, pp=pp, x1p=x1p: e.tensor_add(
            x1p[:, 512:1024], pp[1][:], opb_rep[:, 512:1024]))
        p.emit("vector", lambda e, par=par, x1p=x1p: e.tensor_add(
            x1p[:], x1p[:], xnt2[par][:]), inc=sDVE)
        y_done[i] = sDVE.w()
        dve_x1[i] = sDVE.w()
        p.dma("sync", x1_h[128 * i:128 * (i + 1), :], x1p[:], sST, waits=[dve_x1[i]])
        p.emit("scalar", lambda e, x1p=x1p, par=par: e.activation(
            out=xsq[:], in_=x1p[:], func=AF.Square, accum_out=cs5[par][:, 0:1]),
            waits=[dve_x1[i]], inc=sACT)
        act_sq5[i] = sACT.w()
        p.emit("vector", lambda e, par=par: e.tensor_single_scalar(
            out=ct5[par][:], in_=cs5[par][:], scalar=1.0 / D, op=ALU.mult),
            waits=[act_sq5[i]])
        p.emit("vector", lambda e, par=par: e.tensor_single_scalar(
            out=ct5[par][:], in_=ct5[par][:], scalar=1e-6, op=ALU.add))
        p.emit("vector", lambda e, par=par: e.reciprocal(ct5[par][:], ct5[par][:]),
               inc=sDVE)
        p.emit("scalar", lambda e, par=par: e.activation(
            out=ct5[par][:], in_=ct5[par][:], func=AF.Sqrt),
            waits=[sDVE.w()], inc=sACT)
        wh2 = [sACT.w()]
        if i >= 2:
            wh2.append(st5[i - 2])
        p.emit("vector", lambda e, par=par, x1p=x1p: e.scalar_tensor_tensor(
            out=h2b2[par][:], in0=x1p[:], scalar=ct5[par][:, 0:1], in1=n2w_rep[:],
            op0=ALU.mult, op1=ALU.mult), waits=wh2, inc=sDVE)
        p.dma("sync", h2_h[128 * i:128 * (i + 1), :], h2b2[par][:], sST,
              waits=[sDVE.w()])
        st5[i] = sST.w()
    for _ in gen1:
        pass
    if slots_h is not None:
        p.dma("gpsimd", slots_h.rearrange("(i p) e -> p i e", p=128),
              slots_i[:].rearrange("p (i e) -> p i e", e=E), sLD2,
              waits=[p7a["nslots"]])
    if nphase <= 5:
        p.run_block()
        scope["es"].close()
        ctx.close()
        return nc

    TB = min(1024, L)
    NTB = L // TB
    new_phase()
    wbctx.close()
    h2T = [sbl(f"h2T{g}", [128, TB], BF16) for g in range(NG)]
    hgf = [sbl(f"hgf{f}", [128, TB], BF16) for f in range(NFT)]
    gw_r = [sbl(f"gw_r{g}", [128, DI], BF16) for g in range(NG)]
    uw_r = [sbl(f"uw_r{g}", [128, DI], BF16) for g in range(NG)]
    dw_r = [sbl(f"dw_r{f}", [128, D], BF16) for f in range(NFT)]
    sg_t = sbl("sg_t", [128, 512], BF16)
    xres = [sbl(f"xres{i}", [128, D], F32) for i in range(2)]
    x1res = [sbl(f"x1res{i}", [128, D], F32) for i in range(2)]
    opt_t = [sbl(f"opt_t{i}", [128, D], F32) for i in range(2)]

    for g in range(NG):
        p.dma("gpsimd", gw_r[g][:], gate_wT[128 * g:128 * (g + 1), :], sLD2)
        p.dma("gpsimd", uw_r[g][:], up_wT[128 * g:128 * (g + 1), :], sLD2)
    for f in range(NFT):
        p.dma("gpsimd", dw_r[f][:], down_wT[128 * f:128 * (f + 1), :], sLD2)
    nw6 = sLD2.w()
    comb = [sbl(f"comb{i}", [128, D + 16], BF16) for i in range(2)]
    ghi32 = sbl("ghi32", [128, E], F32)
    glo32 = sbl("glo32", [128, E], F32)
    bnd_reg = nc.gpsimd.alloc_register("bnd_reg")
    bnd2_reg = nc.gpsimd.alloc_register("bnd2_reg")
    p.emit("gpsimd", lambda e: e.reg_mov(bnd_reg, E * CAP - 1), selfsync=False)
    p.emit("gpsimd", lambda e: e.reg_mov(bnd2_reg, E * CAP), selfsync=False,
           inc=sPOOL)
    zrow = sbl("zrow", [1, D], BF16)
    p.emit("gpsimd", lambda e: e.memset(zrow[:], 0.0), inc=sPOOL, selfsync=False)
    p.dma("gpsimd", xmlp_h[E * CAP:E * CAP + 1, :], zrow[0:1, :], sW12,
          waits=[sPOOL.w()])

    def p7a_scatter():
        for i in range(NT128):
            par = i % 2
            wc = [sSCAT[par].w()] if i >= 2 else [p7a["nslots"]]
            p.dma("gpsimd", comb[par][:, 0:D], xn_f32[128 * i:128 * (i + 1), :],
                  sCOMB[par], waits=wc)
            nld = sCOMB[par].w()
            p.emit("vector", lambda e, i=i, par=par: e.tensor_copy(
                out=comb[par][:, D:D + 8], in_=probs_sb[:, E * i:E * (i + 1)]),
                waits=[nld])
            p.emit("vector", lambda e, par=par: e.tensor_copy(
                out=ghi32[:], in_=comb[par][:, D:D + 8]))
            p.emit("vector", lambda e, i=i: e.tensor_sub(
                glo32[:], probs_sb[:, E * i:E * (i + 1)], ghi32[:]))
            p.emit("vector", lambda e, par=par: e.tensor_copy(
                out=comb[par][:, D + 8:D + 16], in_=glo32[:]), inc=sDVE)
            ndve = sDVE.w()
            for ee in range(E):
                p.emit("gpsimd", lambda e, i=i, ee=ee, par=par: e.indirect_dma_start(
                    out=xinb_h, out_offset=IndirectOffsetOnAxis(
                        ap=slots_i[:, E * i + ee:E * i + ee + 1], axis=0),
                    in_=comb[par][:], in_offset=None,
                    bounds_check=bnd_reg, oob_is_err=False),
                    waits=([ndve] if ee == 0 else []), inc16=sSCAT[par])
            yield

    gen2 = p7a_scatter()

    def drv2(n=1):
        for _ in range(n):
            if next(gen2, "E") == "E":
                return

    act_ev6 = {}
    dve_hg6 = {}
    pe_gu = {}
    pe_dn = {}
    dve_out = {}
    for tb in range(NTB):
        for g in range(NG):
            p.dma("sync", h2T[g][:], h2_h[TB * tb:TB * (tb + 1), 128 * g:128 * (g + 1)],
                  sH2T, waits=([sST.w(), sPE.w()] if g == 0 else []), transpose=True)
        nld_h2 = sH2T.w()
        NJC = TB // 512
        for f in range(NFT):
            for jc in range(NJC):
                u = f * NJC + jc
                par = u % 2
                gps, ups = PS[par], PS[2 + par]
                wg = [nld_h2, nw6]
                if u >= 2:
                    wg.append(act_ev6[(tb, u - 2)])
                    wg.append(dve_hg6[(tb, u - 2)])
                if tb > 0 and u < 2:
                    wg.append(dve_out[(tb - 1, TB // 128 - 1)])
                for g in range(NG):
                    p.emit("tensor", lambda e, g=g, f=f, jc=jc, gps=gps: e.matmul(
                        gps[:], gw_r[g][:, 128 * f:128 * (f + 1)],
                        h2T[g][:, 512 * jc:512 * (jc + 1)],
                        start=(g == 0), stop=(g == NG - 1)),
                        waits=(wg if g == 0 else []), inc=sPE)
                for g in range(NG):
                    p.emit("tensor", lambda e, g=g, f=f, jc=jc, ups=ups: e.matmul(
                        ups[:], uw_r[g][:, 128 * f:128 * (f + 1)],
                        h2T[g][:, 512 * jc:512 * (jc + 1)],
                        start=(g == 0), stop=(g == NG - 1)), inc=sPE)
                npe = sPE.w()
                p.emit("scalar", lambda e, gps=gps: e.activation(
                    out=sg_t[:], in_=gps[:], func=AF.Silu), waits=[npe], inc=sACT)
                act_ev6[(tb, u)] = sACT.w()
                p.emit("vector", lambda e, f=f, jc=jc, ups=ups: e.tensor_mul(
                    hgf[f][:, 512 * jc:512 * (jc + 1)], sg_t[:], ups[:]),
                    waits=[act_ev6[(tb, u)]], inc=sDVE)
                dve_hg6[(tb, u)] = sDVE.w()
        ndve_hg = sDVE.w()
        for it in range(TB // 128):
            drv2(1)
            i = tb * (TB // 128) + it
            par = it % 2
            wres = [dve_out[(tb, it - 2)]] if it >= 2 else (
                [dve_out[(tb - 1, TB // 128 - 1)]] if tb > 0 else [])
            p.dma("sync", xres[par][:], x_in[128 * i:128 * (i + 1), :], sRES[par],
                  waits=wres)
            p.dma("sync", x1res[par][:], x1_h[128 * i:128 * (i + 1), :], sRES[par])
            nld_r = sRES[par].w()
            wd = [ndve_hg]
            if it >= 2:
                wd.append(dve_out[(tb, it - 2)])
            dps = [PS[4], PS[5]] if par == 0 else [bank6, bank7]
            for f in range(NFT):
                for half in range(2):
                    pb = dps[half]
                    p.emit("tensor", lambda e, f=f, it=it, half=half, pb=pb: e.matmul(
                        pb[:], hgf[f][:, 128 * it:128 * (it + 1)],
                        dw_r[f][:, 512 * half:512 * (half + 1)],
                        start=(f == 0), stop=(f == NFT - 1)),
                        waits=(wd if (f == 0 and half == 0) else []),
                        inc=(sPE if f == NFT - 1 else None))
            npe = sPE.w()
            ot = opt_t[par]
            p.emit("vector", lambda e, par=par, ot=ot, dps=dps: e.tensor_add(
                ot[:, 0:512], dps[0][:], xres[par][:, 0:512]),
                waits=[npe, nld_r, sSTO[par].w()])
            p.emit("vector", lambda e, par=par, ot=ot, dps=dps: e.tensor_add(
                ot[:, 512:1024], dps[1][:], xres[par][:, 512:1024]))
            p.emit("vector", lambda e, par=par, ot=ot: e.tensor_add(
                ot[:], ot[:], x1res[par][:]), inc=sDVE)
            dve_out[(tb, it)] = sDVE.w()
            p.dma("sync", outp_h[128 * i:128 * (i + 1), :], ot[:], sSTO[par],
                  waits=[dve_out[(tb, it)]])
    for _ in gen2:
        pass
    nscat = [sSCAT[0].w(), sSCAT[1].w()]
    if nphase <= 6:
        p.run_block()
        scope["es"].close()
        ctx.close()
        return nc

    if nphase <= 7:
        p.run_block()
        scope["es"].close()
        ctx.close()
        return nc

    new_phase()
    G_e = [sbl(f"G_e{i}", [128, CAP // 128], F32) for i in range(2)]
    gtmp = [sbl(f"gtmp{i}", [128, 16], BF16) for i in range(2)]
    xinT = [sbl(f"xinT{g}", [128, CAP], BF16) for g in range(NG)]
    he = [sbl(f"he{f}", [128, CAP], BF16) for f in range(NFT)]
    w1sb = [sbl(f"w1sb{g}", [128, DI], BF16) for g in range(NG)]
    w2sb = [sbl(f"w2sb{f}", [128, D], BF16) for f in range(NFT)]
    xmt = sbl("xmt", [128, D], BF16)
    JW = min(512, CAP)
    NJ7 = CAP // 128
    pe_he_end = {}
    for e_i in range(E):
        epar = e_i % 2
        ge = G_e[epar]
        gt = gtmp[epar]
        wtr = nscat + ([pe_he_end[e_i - 1]] if e_i > 0 else [])
        for g in range(NG):
            p.dma("sync", xinT[g][:],
                  xinb_h[CAP * e_i:CAP * (e_i + 1), 128 * g:128 * (g + 1)],
                  sXINT, waits=(wtr if g == 0 else []), transpose=True)
        ntr = sXINT.w()
        for g in range(NG):
            p.dma("gpsimd", w1sb[g][:], w1[e_i, 128 * g:128 * (g + 1), :], sW12,
                  waits=([pe_he_end[e_i - 1]] if (g == 0 and e_i > 0) else []))
        nw1 = sW12.w()
        for f in range(NFT):
            p.dma("gpsimd", w2sb[f][:], w2[e_i, 128 * f:128 * (f + 1), :], sW12,
                  waits=([sPE.w()] if f == 0 else []))
        nw2 = sW12.w()
        wG = nscat if e_i == 0 else []
        for j in range(NJ7):
            p.dma("sync", gt[:, 0:16],
                  xinb_h[CAP * e_i + 128 * j:CAP * e_i + 128 * (j + 1), D:D + 16],
                  sXINT,
                  waits=(wG + [sDVE.w(), sACT.w()] if j == 0 else [sDVE.w()]))
            ng = sXINT.w()
            p.emit("vector", lambda e, j=j, e_i=e_i, ge=ge, gt=gt: e.tensor_copy(
                out=ge[:, j:j + 1], in_=gt[:, e_i:e_i + 1]), waits=[ng])
            p.emit("vector", lambda e, j=j, e_i=e_i, ge=ge, gt=gt: e.tensor_tensor(
                out=ge[:, j:j + 1], in0=ge[:, j:j + 1],
                in1=gt[:, 8 + e_i:8 + e_i + 1], op=ALU.add), inc=sDVE)
        ngd = sDVE.w()
        act_sil = {}
        for jc in range(CAP // JW):
            for f in range(NFT):
                u = jc * NFT + f
                pb = PS[u % 4]
                wh = [ntr, nw1, sACT.w()] if u == 0 else (
                    [act_sil[u - 4]] if u >= 4 else [])
                for g in range(NG):
                    p.emit("tensor", lambda e, f=f, g=g, jc=jc, pb=pb: e.matmul(
                        pb[:, 0:JW], w1sb[g][:, 128 * f:128 * (f + 1)],
                        xinT[g][:, JW * jc:JW * (jc + 1)],
                        start=(g == 0), stop=(g == NG - 1)),
                        waits=(wh if g == 0 else []),
                        inc=(sPE if g == NG - 1 else None))
                p.emit("scalar", lambda e, f=f, jc=jc, pb=pb: e.activation(
                    out=he[f][:, JW * jc:JW * (jc + 1)], in_=pb[:, 0:JW],
                    func=AF.Silu), waits=[sPE.w()], inc=sACT)
                act_sil[u] = sACT.w()
        nhe_all = sACT.w()
        pe_he_end[e_i] = sPE.w()
        for j in range(NJ7):
            dps2 = [PS[4], PS[5]] if j % 2 == 0 else [bank6, bank7]
            for half in range(2):
                pb = dps2[half]
                for f in range(NFT):
                    p.emit("tensor", lambda e, f=f, j=j, half=half, pb=pb: e.matmul(
                        pb[:], he[f][:, 128 * j:128 * (j + 1)],
                        w2sb[f][:, 512 * half:512 * (half + 1)],
                        start=(f == 0), stop=(f == NFT - 1)),
                        waits=(([nhe_all, nw2, ngd, sACT.w()] if j == 0 else
                                [sACT.w()]) if (f == 0 and half == 0) else []),
                        inc=(sPE if f == NFT - 1 else None))
            npe2 = sPE.w()
            p.emit("scalar", lambda e, j=j, dps2=dps2, ge=ge: e.activation(
                out=xmt[:, 0:512], in_=dps2[0][:], func=AF.Identity,
                scale=ge[:, j:j + 1]), waits=[npe2, sST.w()])
            p.emit("scalar", lambda e, j=j, dps2=dps2, ge=ge: e.activation(
                out=xmt[:, 512:1024], in_=dps2[1][:], func=AF.Identity,
                scale=ge[:, j:j + 1]), inc=sACT)
            p.dma("sync", xmlp_h[CAP * e_i + 128 * j:CAP * e_i + 128 * (j + 1), :],
                  xmt[:], sST, waits=[sACT.w()])

    new_phase()
    gat = [[sbl(f"gat{par}_{ee}", [128, D], BF16) for ee in range(E)] for par in (0, 1)]
    opf = [sbl(f"opf{i}", [128, D], F32) for i in range(2)]
    oacc = [sbl(f"oacc{i}", [128, D], F32) for i in range(2)]
    st_all = sST.w()
    sto_all = [sSTO[0].w(), sSTO[1].w()]
    out_done = {}
    gat_done = {}
    pe8 = {}
    for i in range(NT128):
        par = i % 2
        wts = [out_done[i - 2], sSTO[par].w()] if i >= 2 else [st_all] + sto_all
        p.dma("sync", opf[par][:], outp_h[128 * i:128 * (i + 1), :], sOPF[par], waits=wts)
        wg = [pe8[i - 2]] if i >= 2 else [st_all] + sto_all
        for ee in range(E):
            p.emit("gpsimd", lambda e, i=i, ee=ee, par=par: e.indirect_dma_start(
                out=gat[par][ee][:], out_offset=None,
                in_=xmlp_h, in_offset=IndirectOffsetOnAxis(
                    ap=slots_g[:, E * i + ee:E * i + ee + 1], axis=0),
                bounds_check=bnd2_reg, oob_is_err=False),
                waits=(wg if ee == 0 else []), inc16=sGAT)
        gat_done[i] = sGAT.w()
        pw8 = PSW[par]
        wmm8 = [gat_done[i]]
        if i >= 2:
            wmm8.append(out_done[i - 2])
        for ee in range(E):
            for half in range(2):
                p.emit("tensor", lambda e, ee=ee, half=half, par=par, pw8=pw8: e.matmul(
                    pw8[:, 512 * half:512 * (half + 1)], ident_b[:],
                    gat[par][ee][:, 512 * half:512 * (half + 1)],
                    start=(ee == 0), stop=(ee == E - 1)),
                    waits=(wmm8 if (ee == 0 and half == 0) else []),
                    inc=(sPE if (ee == E - 1 and half == 1) else None))
        pe8[i] = sPE.w()
        oc = oacc[par]
        p.emit("vector", lambda e, par=par, oc=oc, pw8=pw8: e.tensor_add(
            oc[:], pw8[:], opf[par][:]),
            waits=[sOPF[par].w(), pe8[i], sSTO[par].w()], inc=sDVE)
        out_done[i] = sDVE.w()
        p.dma("sync", out_hbm[128 * i:128 * (i + 1), :], oc[:], sSTO[par],
              waits=[out_done[i]])
    p.emit("gpsimd", lambda e: e.memset(fin_t[:], 0.0),
           waits=[sSTO[0].w(), sSTO[1].w(), sST.w()], inc=sFIN, selfsync=False)

    p.run_block()
    if scope["es"] is not None:
        scope["es"].close()
    ctx.close()
    return nc


def make_host_inputs(inputs, L=8192):
    f = lambda a: np.ascontiguousarray(np.asarray(a, dtype=np.float32))
    onehot = np.zeros((128, DS * 128), np.float32)
    for s in range(DS):
        onehot[s, 128 * s:128 * s + 64] = 1.0
        onehot[64 + s, 128 * s + 64:128 * s + 128] = 1.0
    LT = np.triu(np.ones((128, 128), np.float32))
    common = {
        "pw_wT": f(np.asarray(inputs["L0_attn_pw_w"]).T),
        "out_projT": f(np.asarray(inputs["L0_attn_out_proj_w"]).T),
        "gate_wT": f(np.asarray(inputs["L0_attn_gate_w"]).T),
        "up_wT": f(np.asarray(inputs["L0_attn_up_w"]).T),
        "down_wT": f(np.asarray(inputs["L0_attn_down_w"]).T),
        "choice_wT": f(np.asarray(inputs["L0_ffn_choice_w"]).T),
        "xpwT": f(np.asarray(inputs["L0_attn_x_proj_w"]).T),
        "xpw_dt": f(np.asarray(inputs["L0_attn_x_proj_w"])[0:DH, :]),
        "dtwT": f(np.asarray(inputs["L0_attn_dt_proj_w"]).T),
        "dt_b": f(np.asarray(inputs["L0_attn_dt_proj_b"]).reshape(DH, 1)),
        "A_log_t": f(inputs["L0_attn_A_log"]),
        "dwk": f(np.asarray(inputs["L0_attn_dw_kernel"]).reshape(D, KC)),
        "dw_bias": f(np.asarray(inputs["L0_attn_dw_bias"]).reshape(D, 1)),
        "pw_b": f(np.asarray(inputs["L0_attn_pw_b"]).reshape(D, 1)),
        "opb": f(np.asarray(inputs["L0_attn_out_proj_b"]).reshape(1, D)),
        "n1w": f(np.asarray(inputs["L0_attn_norm1_w"]).reshape(D, 1)),
        "n2w": f(np.asarray(inputs["L0_attn_norm2_w"]).reshape(1, D)),
        "w1": f(inputs["L0_ffn_w1"]),
        "w2": f(inputs["L0_ffn_w2"]),
        "c_ident": np.eye(128, dtype=np.float32),
        "c_onehot": onehot,
        "c_LT": np.ascontiguousarray(LT),
        "c_ones": np.ones((128, 1), np.float32),
        "c_ecap": (np.arange(E, dtype=np.float32) * (L // 4)).reshape(1, E),
    }
    return common


def kernel(**inputs):
    L = 8192
    x = np.asarray(inputs["L0_attn_x"])
    B = x.shape[0]
    common = make_host_inputs(inputs, L)
    nc = build_program(L=L)
    in_maps = []
    for b in range(B):
        m = dict(common)
        m["x_in"] = np.ascontiguousarray(x[b], dtype=np.float32)
        in_maps.append(m)
    res = run_bass_kernel_spmd(nc, in_maps, core_ids=list(range(B)))
    out = np.stack([np.asarray(res.results[b]["out"]) for b in range(B)], axis=0)
    return out.astype(np.float32)
